# version 01
# baseline (speedup 1.0000x reference)
import numpy as np
import ml_dtypes
import concourse.bass as bass
import concourse.mybir as mybir
from concourse.alu_op_type import AluOpType as OP
from concourse.bass_utils import run_bass_kernel_spmd
from contextlib import ExitStack

F32 = mybir.dt.float32
BF16 = mybir.dt.bfloat16
FP8 = mybir.dt.float8e4
I32 = mybir.dt.int32
AF = mybir.ActivationFunctionType
AX = mybir.AxisListType.X

L = 2048
D = 1024
H = 8
HD = 128
NT = L // 128
KD = D // 128
FF = 4096
NM = FF // 128
TT = 256
NFT = L // TT

ENGS = ("T", "V", "A", "S", "G")
_ENG_ATTR = {"T": "tensor", "V": "vector", "A": "scalar", "S": "sync", "G": "gpsimd"}


class EB:
    def __init__(self):
        self.ops = {e: [] for e in ENGS}
        self.cnt = {e: 0 for e in ENGS}

    def emit(self, e, fn, deps=(), inc=0, hard=()):
        deps = tuple((s, v) for (s, v) in deps if s is not None and s != e and v > 0)
        deps = deps + tuple(t for t in hard if t is not None and t[1] > 0)
        self.ops[e].append((fn, deps, inc))
        if inc:
            self.cnt[e] += inc
        return (e, self.cnt[e])

    def t(self, fn, deps=(), hard=()):
        return self.emit("T", fn, deps, 1, hard)

    def v(self, fn, deps=(), hard=()):
        return self.emit("V", fn, deps, 1, hard)

    def a(self, fn, deps=(), hard=()):
        return self.emit("A", fn, deps, 1, hard)

    def s(self, fn, deps=(), hard=()):
        return self.emit("S", fn, deps, 16, hard)

    def g(self, fn, deps=(), hard=()):
        return self.emit("G", fn, deps, 16, hard)

    def build(self, block, sems):
        for e in ENGS:
            ops = self.ops[e]
            sem_self = sems[e]

            def mk(ops=ops, sem_self=sem_self):
                def body(eng):
                    waited = {}
                    for fn, deps, inc in ops:
                        for (s, v) in deps:
                            if waited.get(s, 0) < v:
                                eng.wait_ge(sems[s], v)
                                waited[s] = v
                        ins = fn(eng)
                        if inc:
                            ins.then_inc(sem_self, inc)
                return body

            getattr(block, _ENG_ATTR[e])(mk())


def const_inputs():
    sl = -np.tril(np.ones((128, 128), np.float32), -1)
    su = -np.triu(np.ones((128, 128), np.float32), 1)
    return {
        "c_ident_bf": np.eye(128, dtype=np.float32).astype(ml_dtypes.bfloat16),
        "c_ident_f": np.eye(128, dtype=np.float32),
        "c_ones_f": np.ones((128, 128), np.float32),
        "c_mask_nn": np.concatenate([sl, su], 1).astype(ml_dtypes.bfloat16),
        "c_mask_ui": np.triu(np.ones((128, 128), np.float32), 0).astype(ml_dtypes.bfloat16),
    }


def build_nc(debug=()):
    nc = bass.Bass("TRN2", target_bir_lowering=False)

    x_d = nc.dram_tensor("x", [L, D], F32, kind="ExternalInput").ap()
    wq_d = nc.dram_tensor("wq", [D, D], F32, kind="ExternalInput").ap()
    wk_d = nc.dram_tensor("wk", [D, D], F32, kind="ExternalInput").ap()
    wv_d = nc.dram_tensor("wv", [D, D], F32, kind="ExternalInput").ap()
    wb_d = nc.dram_tensor("wb", [D, H], F32, kind="ExternalInput").ap()
    lnw_d = nc.dram_tensor("lnw", [1, D], F32, kind="ExternalInput").ap()
    lnb_d = nc.dram_tensor("lnb", [1, D], F32, kind="ExternalInput").ap()
    w1_d = nc.dram_tensor("w1", [FF, D], F32, kind="ExternalInput").ap()
    cw_d = nc.dram_tensor("cw", [FF, 3], F32, kind="ExternalInput").ap()
    cb_d = nc.dram_tensor("cb", [FF, 1], F32, kind="ExternalInput").ap()
    w2_d = nc.dram_tensor("w2", [D, FF], F32, kind="ExternalInput").ap()
    identbf_d = nc.dram_tensor("c_ident_bf", [128, 128], BF16, kind="ExternalInput").ap()
    identf_d = nc.dram_tensor("c_ident_f", [128, 128], F32, kind="ExternalInput").ap()
    onesf_d = nc.dram_tensor("c_ones_f", [128, 128], F32, kind="ExternalInput").ap()
    masknn_d = nc.dram_tensor("c_mask_nn", [128, 256], BF16, kind="ExternalInput").ap()
    maskui_d = nc.dram_tensor("c_mask_ui", [128, 128], BF16, kind="ExternalInput").ap()
    x1sp_d = nc.dram_tensor("x1sp", [L, D], F32, kind="Internal").ap()
    xqsp_d = nc.dram_tensor("xqsp", [KD * 128, L], BF16, kind="Internal").ap()
    out_d = nc.dram_tensor("out", [L, D], F32, kind="ExternalOutput").ap()

    dbg = {}
    for name in debug:
        shp = {"xfm": ([128, KD * L], BF16), "qdm": ([128, H * L], BF16),
               "ktm": ([128, H * L], BF16), "vtm": ([128, H * L], BF16),
               "beta": ([128, NT * H], F32), "otm": ([128, NT * D], F32),
               "w1qt": ([128, KD * FF], FP8), "w2qt": ([128, NM * D], BF16),
               "corr1": ([128, NM], F32), "wsc": ([128, 8], F32),
               "h2q": ([128, NM * TT], BF16), "ht": ([128, NM * 258], BF16),
               "xft": ([128, KD * 258], BF16), "xqtr": ([128, D], BF16),
               "x1c": ([128, D], F32), "oln": ([128, D], F32),
               "cols": ([128, 64], F32), "otm0": ([128, D], F32),
               "d1c": ([128, D], F32), "n1c": ([128, D], F32),
               "lnwb": ([128, D], F32), "lnbb": ([128, D], F32),
               "xqq": ([128, D], BF16), "qscq": ([128, 512], F32),
               "cfsq": ([128, 512], F32), "bsq": ([128, 512], F32),
               "n3c": ([128, D], F32), "s1c": ([128, D], F32)}[name]
        dbg[name] = nc.dram_tensor("dbg_" + name, shp[0], shp[1], kind="ExternalOutput").ap()

    b = EB()
    x_r = x_d.rearrange("(t p) d -> p t d", p=128)
    x1r = x1sp_d.rearrange("(t p) d -> p t d", p=128)
    xq_r = xqsp_d.rearrange("(k p) t -> p k t", p=128)
    out_r = out_d.rearrange("(t p) d -> p t d", p=128)

    es_glob = ExitStack()
    sb = lambda n, s, d: es_glob.enter_context(nc.sbuf_tensor(n, s, d, side="right"))
    IBF = sb("ibf", [128, 128], BF16)
    IF = sb("if_", [128, 128], F32)
    ONF = sb("onf", [128, 128], F32)
    MNN = sb("mnn", [128, 256], BF16)
    MUI = sb("mui", [128, 128], BF16)
    BETA = sb("betap", [128, NT * H], F32)
    BETAN = sb("betan", [128, NT * H], F32)
    COLS = sb("colsb", [128, 64], F32)
    WSC = sb("wscb", [128, 8], F32)
    ACC = sb("accb", [128, 64], F32)
    SCRP = sb("scrp", [128, 4], F32)
    PS = [es_glob.enter_context(nc.psum_tensor("ps%d" % i, [128, 512], F32))
          for i in range(8)]

    with ExitStack() as es_run:
        sems = {e: es_run.enter_context(nc.semaphore("sem" + e)) for e in ENGS}

        es_p = ExitStack()
        sbp = lambda n, s, d: es_p.enter_context(nc.sbuf_tensor(n, s, d))
        XTMB = sbp("xtmb", [128, 1024], BF16)
        WQB = sbp("wqb", [128, KD * D], BF16)
        WKB = sbp("wkb", [128, KD * D], BF16)
        WVB = sbp("wvb", [128, KD * D], BF16)
        WBB = sbp("wbb", [128, KD * H], BF16)
        KSTG = sbp("kstg", [128, D], F32)
        SQS = sbp("sqs", [128, 128], F32)
        KNRM = sbp("knrm", [128, 16], F32)
        WMT = sbp("wmt", [128, 1024], F32)
        es_xfm = ExitStack()
        XFM = es_xfm.enter_context(nc.sbuf_tensor("xfm", [128, KD * L], BF16))
        es_late = ExitStack()
        I32T = es_late.enter_context(nc.sbuf_tensor("i32t", [128, 512], I32, side="right"))
        GBRP = es_late.enter_context(nc.sbuf_tensor("gbrp", [128, 2056], F32, side="right"))
        DGS = es_late.enter_context(nc.sbuf_tensor("dgs", [128, 128], F32, side="right"))
        es_qkv = ExitStack()
        QDM = es_qkv.enter_context(nc.sbuf_tensor("qdm", [128, H * L], BF16, side="right"))
        KTM = es_qkv.enter_context(nc.sbuf_tensor("ktm", [128, H * L], BF16, side="right"))
        VTM = es_qkv.enter_context(nc.sbuf_tensor("vtm", [128, H * L], BF16, side="right"))

        b.v(lambda e: e.memset(SCRP[:, :], 1.0))

        def vsp():
            b.v(lambda e: e.tensor_scalar(
                SCRP[:, 0:1], SCRP[:, 0:1], 1.0, None, op0=OP.mult))

        t_ib = b.s(lambda e: e.dma_start(out=IBF[:, :], in_=identbf_d[:, :]))
        t_if = b.s(lambda e: e.dma_start(out=IF[:, :], in_=identf_d[:, :]))
        t_on = b.s(lambda e: e.dma_start(out=ONF[:, :], in_=onesf_d[:, :]))
        t_mn = b.s(lambda e: e.dma_start(out=MNN[:, :], in_=masknn_d[:, :]))
        t_mu = b.s(lambda e: e.dma_start(out=MUI[:, :], in_=maskui_d[:, :]))
        for wd, wt in ((wq_d, WQB), (wk_d, WKB), (wv_d, WVB)):
            b.g(lambda e, wd=wd, wt=wt: e.dma_start(
                out=wt[:, :].rearrange("p (k d) -> p k d", k=KD),
                in_=wd.rearrange("(k p) d -> p k d", p=128)))
        t_wload = b.g(lambda e: e.dma_start(
            out=WBB[:, :].rearrange("p (k h) -> p k h", k=KD),
            in_=wb_d.rearrange("(k p) h -> p k h", p=128)))

        xfm_r = XFM[:, :].rearrange("p (k t) -> p k t", k=KD)
        tr_ev = {}
        xld = {0: b.g(lambda e: e.dma_start(out=XTMB[:, :], in_=x_r[:, 0, :]))}
        for c in range(NT):
            last_mm = None
            for half in range(2):
                bank = PS[half]
                tk = None
                for kk in range(4):
                    k = half * 4 + kk
                    def fn(e, k=k, kk=kk, bank=bank):
                        return e.matmul(bank[:, 128 * kk:128 * kk + 128],
                                        XTMB[:, 128 * k:128 * k + 128],
                                        IBF[:, :], start=True, stop=True)
                    deps = [xld[c], t_ib]
                    if (c - 1, half) in tr_ev:
                        deps.append(tr_ev[(c - 1, half)])
                    tk = b.t(fn, deps=tuple(deps))
                last_mm = tk
                def ev(e, c=c, half=half, bank=bank):
                    return e.tensor_copy(
                        xfm_r[:, 4 * half:4 * half + 4, 128 * c:128 * c + 128],
                        bank[:, :].rearrange("p (k t) -> p k t", k=4))
                tr_ev[(c, half)] = b.v(ev, deps=(tk,))
            if c + 1 < NT:
                xld[c + 1] = b.g(lambda e, c=c: e.dma_start(
                    out=XTMB[:, :], in_=x_r[:, c + 1, :]), deps=(last_mm,))
        t_xfm_all = tr_ev[(NT - 1, 1)]

        wm_toks = []
        prev_ld = None
        for base, wd, ntile in ((0, w1_d, 32), (32, w2_d, 32)):
            if base == 0:
                wr = wd.rearrange("(r p) d -> p r d", p=128)
                srcs = [(r, None) for r in range(32)]
            else:
                wr = wd.rearrange("(r p) f -> p r f", p=128)
                srcs = [(r // 4, 1024 * (r % 4)) for r in range(32)]
            for i, (r, coff) in enumerate(srcs):
                def ld(e, wr=wr, r=r, coff=coff):
                    src = wr[:, r, :] if coff is None else wr[:, r, coff:coff + 1024]
                    return e.dma_start(out=WMT[:, :], in_=src)
                t_ld = b.s(ld, deps=(prev_ld,) if prev_ld else ())
                t_ab = b.a(lambda e, base=base, i=i: e.activation(
                    WMT[:, :], WMT[:, :], AF.Abs,
                    accum_out=ACC[:, base + i:base + i + 1]), deps=(t_ld,))
                wm_toks.append(t_ab)
                prev_ld = t_ab
        t_wm1 = wm_toks[31]
        t_wm2 = wm_toks[63]

        qdm_toks = {}
        for h in range(H):
            for n in range(4):
                idx = h * 4 + n
                bank = PS[2 + idx % 2]
                tk = None
                for k in range(KD):
                    def fn(e, h=h, n=n, k=k, bank=bank):
                        return e.matmul(bank[:, :],
                                        WQB[:, D * k + 128 * h:D * k + 128 * h + 128],
                                        XFM[:, L * k + 512 * n:L * k + 512 * n + 512],
                                        start=(k == 0), stop=(k == KD - 1))
                    deps = [t_xfm_all, t_wload]
                    if idx >= 2:
                        deps.append(qdm_toks[idx - 2])
                    tk = b.t(fn, deps=tuple(deps))
                qdm_toks[idx] = b.v(
                    lambda e, h=h, n=n, bank=bank: e.tensor_copy(
                        QDM[:, L * h + 512 * n:L * h + 512 * n + 512], bank[:, :]),
                    deps=(tk,))
        t_q_all = qdm_toks[H * 4 - 1]

        ktm_r = KTM[:, :].rearrange("p (h c) -> p h c", h=H)
        k_ev = {}
        k_wr = {}
        for c in range(NT):
            for n2 in range(2):
                bank = PS[4 + n2]
                tk = None
                for k in range(KD):
                    def fn(e, c=c, n2=n2, k=k, bank=bank):
                        return e.matmul(bank[:, :],
                                        XFM[:, L * k + 128 * c:L * k + 128 * c + 128],
                                        WKB[:, D * k + 512 * n2:D * k + 512 * n2 + 512],
                                        start=(k == 0), stop=(k == KD - 1))
                    deps = [t_xfm_all]
                    if c >= 1:
                        deps.append(k_ev[(c - 1, n2)])
                    tk = b.t(fn, deps=tuple(deps))
                deps_e = [tk]
                if c >= 1:
                    deps_e.append(k_wr[c - 1])
                k_ev[(c, n2)] = b.v(
                    lambda e, n2=n2, bank=bank: e.tensor_copy(
                        KSTG[:, 512 * n2:512 * n2 + 512], bank[:, :]),
                    deps=tuple(deps_e))
            sq_toks = []
            for h in range(H):
                sq_toks.append(b.a(
                    lambda e, h=h: e.activation(
                        SQS[:, :], KSTG[:, 128 * h:128 * h + 128],
                        AF.Square, accum_out=KNRM[:, h:h + 1]),
                    deps=(k_ev[(c, 0)], k_ev[(c, 1)])))
            t_nf = b.a(lambda e: e.activation(
                KNRM[:, 8:16], KNRM[:, 0:8], AF.Sqrt), deps=tuple(sq_toks))
            t_nm = b.v(lambda e: e.tensor_single_scalar(
                KNRM[:, 8:16], KNRM[:, 8:16], 1e-12, op=OP.max), deps=(t_nf,))
            t_rcp = None
            for h in range(H):
                vsp()
                t_rcp = b.v(lambda e, h=h: e.reciprocal(
                    KNRM[:, h:h + 1], KNRM[:, 8 + h:9 + h]), hard=(t_nm,))
            last = None
            for h in range(H):
                last = b.v(lambda e, h=h, c=c: e.tensor_scalar(
                    ktm_r[:, h, 128 * c:128 * c + 128],
                    KSTG[:, 128 * h:128 * h + 128],
                    KNRM[:, h:h + 1], None, op0=OP.mult), hard=(t_rcp,))
            k_wr[c] = last
        t_k_all = k_wr[NT - 1]

        vtm_r = VTM[:, :].rearrange("p (h c) -> p h c", h=H)
        v_ev = {}
        for c in range(NT):
            for n2 in range(2):
                bank = PS[2 + n2]
                tk = None
                for k in range(KD):
                    def fn(e, c=c, n2=n2, k=k, bank=bank):
                        return e.matmul(bank[:, :],
                                        XFM[:, L * k + 128 * c:L * k + 128 * c + 128],
                                        WVB[:, D * k + 512 * n2:D * k + 512 * n2 + 512],
                                        start=(k == 0), stop=(k == KD - 1))
                    deps = [t_xfm_all, t_q_all]
                    if c >= 1:
                        deps.append(v_ev[(c - 1, n2)])
                    tk = b.t(fn, deps=tuple(deps))
                v_ev[(c, n2)] = b.a(
                    lambda e, c=c, n2=n2, bank=bank: e.activation(
                        vtm_r[:, 4 * n2:4 * n2 + 4, 128 * c:128 * c + 128],
                        bank[:, :].rearrange("p (h t) -> p h t", h=4), AF.Silu),
                    deps=(tk,))
        t_v_all = v_ev[(NT - 1, 1)]

        be_toks = []
        for c in range(NT):
            bank = PS[6]
            tk = None
            for k in range(KD):
                def fn(e, c=c, k=k, bank=bank):
                    return e.matmul(bank[:, 0:H],
                                    XFM[:, L * k + 128 * c:L * k + 128 * c + 128],
                                    WBB[:, H * k:H * k + H],
                                    start=(k == 0), stop=(k == KD - 1))
                deps = [t_xfm_all, t_v_all]
                if be_toks:
                    deps.append(be_toks[-1])
                tk = b.t(fn, deps=tuple(deps))
            be_toks.append(b.a(lambda e, c=c, bank=bank: e.activation(
                BETA[:, H * c:H * c + H], bank[:, 0:H], AF.Sigmoid), deps=(tk,)))
        t_beta_all = be_toks[-1]
        t_betan = b.v(lambda e: e.tensor_scalar(
            BETAN[:, :], BETA[:, :], -1.0, None, op0=OP.mult), deps=(t_beta_all,))

        t_r1 = b.v(lambda e: e.reduce_sum(COLS[:, 16:17], ACC[:, 0:32], axis=AX),
                   deps=(t_wm1,))
        t_r2 = b.v(lambda e: e.reduce_sum(COLS[:, 17:18], ACC[:, 32:64], axis=AX),
                   deps=(t_wm2,))
        t_cs = b.t(lambda e: e.matmul(PS[7][0:1, 0:2], ONF[:, 0:1],
                                      COLS[:, 16:18], start=True, stop=True),
                   deps=(t_r1, t_r2, t_on))
        t_wmv = b.v(lambda e: e.tensor_scalar(
            COLS[0:1, 18:20], PS[7][0:1, 0:2], 1.0 / (FF * D), 1e-5,
            op0=OP.mult, op1=OP.max), deps=(t_cs,))
        t_wb2 = b.t(lambda e: e.matmul(PS[7][:, 4:6], ONF[0:1, 0:128],
                                       COLS[0:1, 18:20], start=True, stop=True),
                    deps=(t_wmv,))
        t_wc2 = b.v(lambda e: e.tensor_scalar(
            WSC[:, 0:2], PS[7][:, 4:6], 1.0, None, op0=OP.mult), deps=(t_wb2,))
        vsp()
        b.v(lambda e: e.reciprocal(WSC[:, 2:3], WSC[:, 0:1]),
            deps=(t_wc2,), hard=(t_wc2,))
        vsp()
        t_wsc = b.v(lambda e: e.reciprocal(WSC[:, 3:4], WSC[:, 1:2]))

        if "xfm" in dbg:
            b.g(lambda e: e.dma_start(out=dbg["xfm"][:, :], in_=XFM[:, :]),
                deps=(t_xfm_all,))
        if "qdm" in dbg:
            b.g(lambda e: e.dma_start(out=dbg["qdm"][:, :], in_=QDM[:, :]),
                deps=(t_q_all,))
        if "ktm" in dbg:
            b.g(lambda e: e.dma_start(out=dbg["ktm"][:, :], in_=KTM[:, :]),
                deps=(t_k_all,))
        if "vtm" in dbg:
            b.g(lambda e: e.dma_start(out=dbg["vtm"][:, :], in_=VTM[:, :]),
                deps=(t_v_all,))
        if "beta" in dbg:
            b.g(lambda e: e.dma_start(out=dbg["beta"][:, :], in_=BETA[:, :]),
                deps=(t_beta_all,))

        es_xfm.close()
        es_p.close()

        es_otm = ExitStack()
        OTM = es_otm.enter_context(nc.sbuf_tensor("otm", [128, NT * D], F32, side="right"))

        es_s = ExitStack()
        sbs = lambda n, s, d: es_s.enter_context(nc.sbuf_tensor(n, s, d))
        NL = 4
        S_SB = sbs("s_sb", [128, H * 128], F32)
        S_BF = sbs("s_bfv", [128, H * 128], BF16)
        KKB = sbs("kkb", [128, NL * 256], BF16)
        PPB = sbs("ppb", [128, NL * 512], BF16)
        UBF = sbs("ubf", [128, NL * 128], BF16)
        PPR = sbs("pprb", [128, NL * 128], BF16)
        VBB = sbs("vbb", [128, NL * 128], BF16)
        RBF = sbs("rbfb", [128, NL * 128], BF16)
        DBT = sbs("dbt", [128, NL * 128], BF16)

        deps_in = (t_k_all, t_q_all, t_v_all, t_betan, t_mn, t_mu, t_ib)

        def lane_ops(lane):
            bankA = PS[lane]
            bankU = PS[4 + lane]
            kk = KKB[:, 256 * lane:256 * lane + 256]
            pp = [PPB[:, 512 * lane:512 * lane + 256],
                  PPB[:, 512 * lane + 256:512 * lane + 512]]
            ub = UBF[:, 128 * lane:128 * lane + 128]
            pr = PPR[:, 128 * lane:128 * lane + 128]
            vb = VBB[:, 128 * lane:128 * lane + 128]
            rb = RBF[:, 128 * lane:128 * lane + 128]
            dbta = DBT[:, 128 * lane:128 * lane + 128]
            last_A_read = None
            last_U_read = None
            last_u_ev = None
            for h in (lane, lane + NL):
                ssb = S_SB[:, 128 * h:128 * h + 128]
                sbfh = S_BF[:, 128 * h:128 * h + 128]
                t_sbf = None
                for c in range(NT):
                    ktm_hc = KTM[:, L * h + 128 * c:L * h + 128 * c + 128]
                    qdm_hc = QDM[:, L * h + 128 * c:L * h + 128 * c + 128]
                    vtm_hc = VTM[:, L * h + 128 * c:L * h + 128 * c + 128]
                    bcol = BETA[:, H * c + h:H * c + h + 1]
                    bncol = BETAN[:, H * c + h:H * c + h + 1]

                    t_db = b.v(lambda e, dbta=dbta, bcol=bcol: e.tensor_scalar(
                        dbta, IBF[:, :], bcol, None, op0=OP.mult), deps=deps_in)
                    t_vb = b.v(lambda e, vb=vb, vtm_hc=vtm_hc, bcol=bcol: e.tensor_scalar(
                        vb, vtm_hc, bcol, None, op0=OP.mult), deps=deps_in)
                    yield None
                    d0 = (last_A_read,) if last_A_read else ()
                    b.t(lambda e, bankA=bankA, ktm_hc=ktm_hc: e.matmul(
                        bankA[:, 0:128], ktm_hc, IBF[:, :], start=True, stop=True),
                        deps=d0 + deps_in)
                    t_tr = b.t(lambda e, bankA=bankA, ktm_hc=ktm_hc, dbta=dbta: e.matmul(
                        bankA[:, 128:256], ktm_hc, dbta, start=True, stop=True),
                        deps=(t_db,))
                    t_kk = b.v(lambda e, kk=kk, bankA=bankA: e.tensor_copy(
                        kk, bankA[:, 0:256]), deps=(t_tr,))
                    yield None
                    b.t(lambda e, bankA=bankA, kk=kk: e.matmul(
                        bankA[:, 256:384], kk[:, 128:256], kk[:, 0:128],
                        start=True, stop=True), deps=(t_kk,))
                    t_p0 = b.t(lambda e, bankA=bankA, kk=kk: e.matmul(
                        bankA[:, 384:512], kk[:, 0:128], kk[:, 128:256],
                        start=True, stop=True))
                    t_pp = b.v(lambda e, pp0=pp[0], bankA=bankA: e.tensor_tensor(
                        pp0, bankA[:, 256:512], MNN[:, :], op=OP.mult), deps=(t_p0,))
                    yield None
                    if c > 0:
                        t_ks = b.t(lambda e, bankA=bankA, kk=kk, sbfh=sbfh: e.matmul(
                            bankA[:, 0:128], kk[:, 0:128], sbfh, start=True, stop=True),
                            deps=(t_kk, t_sbf, t_pp))
                        t_rb = b.v(lambda e, rb=rb, bankA=bankA, bncol=bncol, vb=vb:
                                   e.scalar_tensor_tensor(rb, bankA[:, 0:128], bncol, vb,
                                                          op0=OP.mult, op1=OP.add),
                                   deps=(t_ks, t_vb))
                        rsrc = rb
                    else:
                        t_rb = t_vb
                        rsrc = vb
                    yield None
                    d1 = [t_rb]
                    if last_u_ev:
                        d1.append(last_u_ev)
                    if last_U_read:
                        d1.append(last_U_read)
                    t_um = b.t(lambda e, bankU=bankU, rsrc=rsrc: e.matmul(
                        bankU[:, 0:128], IBF[:, :], rsrc, start=True, stop=False),
                        deps=tuple(d1))
                    t_ue = b.a(lambda e, ub=ub, bankU=bankU: e.copy(
                        ub, bankU[:, 0:128]), deps=(t_um,))
                    yield None
                    t_ppk = t_pp
                    for it in range(7):
                        cur = pp[it % 2]
                        t_um = b.t(lambda e, bankU=bankU, cur=cur, ub=ub: e.matmul(
                            bankU[:, 0:128], cur[:, 128:256], ub,
                            start=False, stop=(it == 6)), deps=(t_ue, t_ppk))
                        t_ue = b.a(lambda e, ub=ub, bankU=bankU: e.copy(
                            ub, bankU[:, 0:128]), deps=(t_um,))
                        if it < 6:
                            nxt = pp[(it + 1) % 2]
                            b.t(lambda e, bankA=bankA, cur=cur: e.matmul(
                                bankA[:, 0:128], cur[:, 128:256], cur[:, 0:128],
                                start=True, stop=True), deps=(t_ppk,))
                            t_sq = b.t(lambda e, bankA=bankA, cur=cur: e.matmul(
                                bankA[:, 128:256], cur[:, 0:128], cur[:, 128:256],
                                start=True, stop=True))
                            t_ppk = b.v(lambda e, nxt=nxt, bankA=bankA: e.tensor_copy(
                                nxt, bankA[:, 0:256]), deps=(t_sq,))
                        yield None
                    t_prm = b.t(lambda e, bankA=bankA, kk=kk, qdm_hc=qdm_hc: e.matmul(
                        bankA[:, 0:128], kk[:, 0:128], qdm_hc, start=True, stop=True),
                        deps=(t_ppk,))
                    t_pr = b.v(lambda e, pr=pr, bankA=bankA: e.tensor_tensor(
                        pr, bankA[:, 0:128], MUI[:, :], op=OP.mult), deps=(t_prm,))
                    yield None
                    d2 = (last_U_read,) if last_U_read else ()
                    if c > 0:
                        b.t(lambda e, bankU=bankU, qdm_hc=qdm_hc, sbfh=sbfh: e.matmul(
                            bankU[:, 128:256], qdm_hc, sbfh, start=True, stop=False),
                            deps=(t_sbf, t_ue) + d2)
                        t_om = b.t(lambda e, bankU=bankU, pr=pr, ub=ub: e.matmul(
                            bankU[:, 128:256], pr, ub, start=False, stop=True),
                            deps=(t_pr,))
                    else:
                        t_om = b.t(lambda e, bankU=bankU, pr=pr, ub=ub: e.matmul(
                            bankU[:, 128:256], pr, ub, start=True, stop=True),
                            deps=(t_pr, t_ue) + d2)
                    t_oe = b.a(lambda e, c=c, h=h, bankU=bankU: e.copy(
                        OTM[:, D * c + 128 * h:D * c + 128 * h + 128],
                        bankU[:, 128:256]), deps=(t_om,))
                    last_U_read = t_oe
                    last_u_ev = t_ue
                    yield None
                    t_ds = b.t(lambda e, bankA=bankA, ktm_hc=ktm_hc, ub=ub: e.matmul(
                        bankA[:, 128:256], ktm_hc, ub, start=True, stop=True),
                        deps=(t_pr, t_ue))
                    if c == 0:
                        t_su = b.v(lambda e, ssb=ssb, bankA=bankA: e.tensor_copy(
                            ssb, bankA[:, 128:256]), deps=(t_ds,))
                    else:
                        t_su = b.v(lambda e, ssb=ssb, bankA=bankA: e.tensor_tensor(
                            ssb, ssb, bankA[:, 128:256], op=OP.add), deps=(t_ds,))
                    t_sbf = b.a(lambda e, sbfh=sbfh, ssb=ssb: e.copy(sbfh, ssb),
                                deps=(t_su,))
                    last_A_read = t_su
                    yield (c, h, t_oe, t_sbf)

        lanes = [lane_ops(i) for i in range(NL)]
        o_toks = {}
        sbf_toks = {}
        live = list(range(NL))
        while live:
            nxt = []
            for li in live:
                try:
                    r = next(lanes[li])
                    if r is not None:
                        o_toks[(r[0], r[1])] = r[2]
                        sbf_toks[(r[0], r[1])] = r[3]
                    nxt.append(li)
                except StopIteration:
                    continue
            live = nxt
        scan_done = tuple(sbf_toks[(NT - 1, h)] for h in range(H))
        es_s.close()

        if "otm" in dbg:
            b.s(lambda e: e.dma_start(out=dbg["otm"][:, :], in_=OTM[:, :]),
                deps=tuple(o_toks[(NT - 1, h)] for h in range(H)))

        es_ln = ExitStack()
        sbl = lambda n, s, d: es_ln.enter_context(nc.sbuf_tensor(n, s, d))
        LNWB = sbl("lnwb", [128, D], F32)
        LNBB = sbl("lnbb", [128, D], F32)
        LROW = sbl("lrow", [1, 1024], F32)
        t_lr = b.s(lambda e: e.dma_start(out=LROW[0:1, :], in_=lnw_d[:, :]),
                   deps=scan_done)
        lnb_toks = []
        pb_read = None
        wread = None
        for half in range(2):
            dd = (pb_read,) if pb_read else ()
            tm = b.t(lambda e, half=half: e.matmul(
                PS[6][:, :], ONF[0:1, 0:128], LROW[0:1, 512 * half:512 * half + 512],
                start=True, stop=True), deps=(t_lr, t_on) + dd)
            tv = b.v(lambda e, half=half: e.tensor_copy(
                LNWB[:, 512 * half:512 * half + 512], PS[6][:, :]), deps=(tm,))
            pb_read = tv
            wread = tm
        t_lr2 = b.s(lambda e: e.dma_start(out=LROW[0:1, :], in_=lnb_d[:, :]),
                    deps=(wread, pb_read))
        for half in range(2):
            tm2 = b.t(lambda e, half=half: e.matmul(
                PS[6][:, :], ONF[0:1, 0:128], LROW[0:1, 512 * half:512 * half + 512],
                start=True, stop=True), deps=(t_lr2, pb_read))
            tv2 = b.v(lambda e, half=half: e.tensor_copy(
                LNBB[:, 512 * half:512 * half + 512], PS[6][:, :]), deps=(tm2,))
            pb_read = tv2
            lnb_toks.append(tv2)
        if "lnwb" in dbg:
            b.g(lambda e: e.dma_start(out=dbg["lnwb"][:, :], in_=LNWB[:, :]),
                deps=(lnb_toks[-1],))
            b.g(lambda e: e.dma_start(out=dbg["lnbb"][:, :], in_=LNBB[:, :]),
                deps=(lnb_toks[-1],))
        XRS = sbl("xrs", [128, D], F32)
        X1S = sbl("x1s", [128, D], F32)
        QSC = sbl("qsc", [128, D], F32)
        CFS = sbl("cfs", [128, 512], F32)
        BS_ = sbl("bs_", [128, 512], F32)
        XQB = sbl("xqb", [128, D], BF16)

        t_gz = b.v(lambda e: e.memset(GBRP[:, 0:1], 0.0))
        b.v(lambda e: e.memset(GBRP[:, 2049:2056], 0.0))

        dbg_gate = []
        qsc_rd = None
        x1_toks = []
        xq_toks = []
        gbr_toks = []
        xr_read = None
        x1_read = None
        prev_gm = lnb_toks[-1]
        prev_tr = None
        for c in range(NT):
            deps_o = tuple(o_toks[(c, h)] for h in range(H))
            dd = (xr_read,) if xr_read else scan_done
            t_xr = b.s(lambda e, c=c: e.dma_start(
                out=XRS[:, :], in_=x_r[:, c, :]), deps=dd)
            och = OTM[:, D * c:D * c + D]
            t_sum = b.a(lambda e, och=och: e.activation(
                QSC[:, :], och, AF.Copy, accum_out=COLS[:, 3:4]),
                deps=deps_o + tuple(dbg_gate) + ((qsc_rd,) if qsc_rd else ()))
            t_mu = b.v(lambda e: e.tensor_scalar(
                COLS[:, 3:4], COLS[:, 3:4], 1.0 / D, None, op0=OP.mult),
                deps=(t_sum,))
            if c == 0 and "cols" in dbg:
                dbg_gate.append(b.g(lambda e: e.dma_start(
                    out=dbg["cols"][:, :], in_=COLS[:, :]), deps=(t_mu,)))
                t_gate0 = dbg_gate[-1]
            else:
                t_gate0 = None
            dd = (x1_read,) if x1_read else ()
            vsp()
            t_d1 = b.v(lambda e, och=och: e.tensor_scalar(
                X1S[:, :], och, COLS[:, 3:4], None, op0=OP.subtract),
                deps=dd + tuple(dbg_gate), hard=(t_mu,))
            if c == 0 and "d1c" in dbg:
                t_d1 = b.g(lambda e: e.dma_start(out=dbg["d1c"][:, :], in_=X1S[:, :]),
                           deps=(t_d1,))
            t_ssq = b.a(lambda e: e.activation(
                QSC[:, :], X1S[:, :], AF.Square, accum_out=COLS[:, 4:5]),
                deps=(t_d1,))
            t_var = b.v(lambda e: e.tensor_scalar(
                COLS[:, 5:6], COLS[:, 4:5], 1.0 / D, 1e-5, op0=OP.mult, op1=OP.add),
                deps=(t_ssq,) + ((t_gate0,) if t_gate0 else ()))
            t_sd = b.a(lambda e: e.activation(
                COLS[:, 12:13], COLS[:, 5:6], AF.Sqrt), deps=(t_var,))
            vsp()
            t_rs = b.v(lambda e: e.reciprocal(COLS[:, 6:7], COLS[:, 12:13]), deps=(t_sd,))
            vsp()
            t_n1 = b.v(lambda e: e.tensor_scalar(
                X1S[:, :], X1S[:, :], COLS[:, 6:7], None, op0=OP.mult),
                deps=(t_rs,), hard=(t_rs,))
            if c == 0 and "n1c" in dbg:
                t_n1 = b.g(lambda e: e.dma_start(out=dbg["n1c"][:, :], in_=X1S[:, :]),
                           deps=(t_n1,))
            b.v(lambda e: e.tensor_tensor(
                X1S[:, :], X1S[:, :], LNWB[:, :], op=OP.mult),
                deps=tuple(lnb_toks) + (t_n1,))
            t_n3 = b.v(lambda e: e.tensor_tensor(
                X1S[:, :], X1S[:, :], LNBB[:, :], op=OP.add))
            if c == 0 and "n3c" in dbg:
                t_n3 = b.g(lambda e: e.dma_start(out=dbg["n3c"][:, :], in_=X1S[:, :]),
                           deps=(t_n3,))
            t_s1 = b.v(lambda e: e.tensor_tensor(
                X1S[:, :], X1S[:, :], XRS[:, :], op=OP.add), deps=(t_n3, t_xr))
            xr_read = t_s1
            if c == 0 and "s1c" in dbg:
                t_s1 = b.g(lambda e: e.dma_start(out=dbg["s1c"][:, :], in_=X1S[:, :]),
                           deps=(t_s1,))
            t_su2 = b.a(lambda e: e.activation(
                QSC[:, :], X1S[:, :], AF.Copy, accum_out=COLS[:, 7:8]),
                deps=(t_s1,))
            t_mu2 = b.v(lambda e: e.tensor_scalar(
                COLS[:, 7:8], COLS[:, 7:8], 1.0 / D, None, op0=OP.mult),
                deps=(t_su2,))
            vsp()
            t_d2 = b.v(lambda e: e.tensor_scalar(
                X1S[:, :], X1S[:, :], COLS[:, 7:8], None, op0=OP.subtract),
                hard=(t_mu2,))
            t_ss2 = b.a(lambda e: e.activation(
                QSC[:, :], X1S[:, :], AF.Square, accum_out=COLS[:, 8:9]),
                deps=(t_d2,))
            t_v2 = b.v(lambda e: e.tensor_scalar(
                COLS[:, 9:10], COLS[:, 8:9], 1.0 / D, 1e-5, op0=OP.mult, op1=OP.add),
                deps=(t_ss2,))
            t_sd2 = b.a(lambda e: e.activation(
                COLS[:, 13:14], COLS[:, 9:10], AF.Sqrt), deps=(t_v2,))
            vsp()
            t_rs2 = b.v(lambda e: e.reciprocal(COLS[:, 10:11], COLS[:, 13:14]),
                        deps=(t_sd2,))
            vsp()
            t_x1 = b.v(lambda e: e.tensor_scalar(
                X1S[:, :], X1S[:, :], COLS[:, 10:11], None, op0=OP.mult),
                deps=(t_rs2,), hard=(t_rs2,))
            if c == 0 and "otm0" in dbg:
                dbg_gate.append(b.g(lambda e, och=och: e.dma_start(
                    out=dbg["otm0"][:, :], in_=och), deps=deps_o + (t_x1,)))
            if c == 0 and "x1c" in dbg:
                dbg_gate.append(b.g(lambda e: e.dma_start(
                    out=dbg["x1c"][:, :], in_=X1S[:, :]), deps=(t_x1,)))
            t_sp = b.s(lambda e, c=c: e.dma_start(
                out=x1r[:, c, :], in_=X1S[:, :]), deps=(t_x1,))
            x1_toks.append(t_sp)
            t_am = b.v(lambda e: e.reduce_max(
                COLS[:, 0:1], X1S[:, :], axis=AX,
                apply_absolute_value=True), deps=(t_x1,))
            t_amc = b.v(lambda e: e.tensor_single_scalar(
                COLS[:, 0:1], COLS[:, 0:1], 1e-5, op=OP.max), hard=(t_am,))
            vsp()
            t_rcq = b.v(lambda e: e.reciprocal(COLS[:, 1:2], COLS[:, 0:1]),
                        hard=(t_amc,))
            t_s1m = b.v(lambda e: e.tensor_single_scalar(
                COLS[:, 1:2], COLS[:, 1:2], 127.0, op=OP.mult), hard=(t_rcq,))
            vsp()
            t_g1 = b.v(lambda e: e.tensor_scalar(
                COLS[:, 2:3], COLS[:, 0:1], WSC[:, 0:1], None, op0=OP.mult),
                deps=(t_wsc, t_am), hard=(t_amc,))
            t_g1b = b.v(lambda e: e.tensor_single_scalar(
                COLS[:, 2:3], COLS[:, 2:3], 1.0 / 127.0, op=OP.mult), hard=(t_g1,))
            vsp()
            t_dg = b.v(lambda e: e.tensor_scalar(
                DGS[:, :], IF[:, :], COLS[:, 2:3], None, op0=OP.mult),
                deps=(t_g1b, t_if, prev_gm), hard=(t_g1b,))
            t_gm = b.t(lambda e: e.matmul(
                PS[6][:, 0:128], ONF[:, :], DGS[:, :], start=True, stop=True),
                deps=(t_dg, prev_gm))
            t_gb = b.v(lambda e, c=c: e.tensor_copy(
                GBRP[:, 1 + 128 * c:1 + 128 * c + 128], PS[6][:, 0:128]), deps=(t_gm,))
            prev_gm = t_gb
            gbr_toks.append(t_gb)
            q_half = []
            for hf in range(2):
                dvv = [t_s1m] + ([qsc_rd] if qsc_rd else [])
                t_vv = b.a(lambda e, hf=hf: e.activation(
                    QSC[:, 0:512], X1S[:, 512 * hf:512 * hf + 512],
                    AF.Copy, bias=0.0, scale=COLS[:, 1:2]), deps=tuple(dvv))
                t_i3 = b.v(lambda e: e.tensor_scalar(
                    I32T[:, 0:512], QSC[:, 0:512], 128.5, None, op0=OP.add),
                    deps=(t_vv,))
                t_cf = b.v(lambda e: e.tensor_copy(CFS[:, :], I32T[:, 0:512]),
                           hard=(t_i3,))
                qsc_rd = b.v(lambda e: e.scalar_tensor_tensor(
                    BS_[:, :], CFS[:, :], -128.5, QSC[:, 0:512],
                    op0=OP.add, op1=OP.is_gt), hard=(t_cf,))
                q_half.append(b.v(lambda e, hf=hf: e.scalar_tensor_tensor(
                    XQB[:, 512 * hf:512 * hf + 512], CFS[:, :], 256.0, BS_[:, :],
                    op0=OP.min, op1=OP.subtract), hard=(qsc_rd,)))
            if c == 0 and "xqq" in dbg:
                q_half[1] = b.g(lambda e: e.dma_start(
                    out=dbg["xqq"][:, :], in_=XQB[:, :]), deps=tuple(q_half))
                dbg_gate.append(q_half[1])
            if c == 0 and "qscq" in dbg:
                q_half[1] = b.g(lambda e: e.dma_start(
                    out=dbg["qscq"][:, :], in_=QSC[:, 0:512]), deps=tuple(q_half))
                dbg_gate.append(q_half[1])
            if c == 0 and "cfsq" in dbg:
                q_half[1] = b.g(lambda e: e.dma_start(
                    out=dbg["cfsq"][:, :], in_=CFS[:, :]), deps=tuple(q_half))
                dbg_gate.append(q_half[1])
            if c == 0 and "bsq" in dbg:
                q_half[1] = b.g(lambda e: e.dma_start(
                    out=dbg["bsq"][:, :], in_=BS_[:, :]), deps=tuple(q_half))
                dbg_gate.append(q_half[1])
            tr_last = []
            for half in range(2):
                tk = None
                for kk2 in range(4):
                    k = 4 * half + kk2
                    dd2 = [q_half[1]]
                    if kk2 == 0:
                        if half == 1:
                            dd2.append(tr_last[0])
                        elif prev_tr:
                            dd2.append(prev_tr)
                    tk = b.t(lambda e, k=k, kk2=kk2: e.matmul(
                        PS[7][:, 128 * kk2:128 * kk2 + 128],
                        XQB[:, 128 * k:128 * k + 128], IBF[:, :],
                        start=True, stop=True), deps=tuple(dd2))
                tr_last.append(b.v(lambda e, half=half: e.tensor_copy(
                    XQB[:, 512 * half:512 * half + 512], PS[7][:, :]), deps=(tk,)))
            prev_tr = tr_last[1]
            if c == 0 and "xqtr" in dbg:
                prev_tr = b.g(lambda e: e.dma_start(
                    out=dbg["xqtr"][:, :], in_=XQB[:, :]),
                    deps=(tr_last[0], tr_last[1]))
            t_xq = b.s(lambda e, c=c: e.dma_start(
                out=xq_r[:, :, 128 * c:128 * c + 128],
                in_=XQB[:, :].rearrange("p (k t) -> p k t", k=KD)),
                deps=(tr_last[0], tr_last[1]))
            x1_read = t_xq
            xq_toks.append(t_xq)
        t_xq_all = xq_toks[-1]
        es_ln.close()
        es_otm.close()
        es_qkv.close()

        es_wq = ExitStack()
        W1QT = es_wq.enter_context(nc.sbuf_tensor("w1qt", [128, KD * FF], FP8, side="right"))
        W2QT = es_wq.enter_context(nc.sbuf_tensor("w2qt", [128, NM * D], BF16, side="right"))
        es_corr = ExitStack()
        sbc = lambda n, s, d: es_corr.enter_context(nc.sbuf_tensor(n, s, d, side="right"))
        CORR1 = sbc("corr1", [128, NM], F32)
        CORR2B = sbc("corr2b", [128, D], F32)
        ONEC = sbc("onec", [128, 1], BF16)
        CWC = sbc("cwc", [128, NM * 3], F32)
        CBC = sbc("cbc", [128, NM], F32)
        es_wt = ExitStack()
        sbw = lambda n, s, d: es_wt.enter_context(nc.sbuf_tensor(n, s, d))
        WST = sbw("wst", [128, 4096], F32)
        WQS = sbw("wqs", [128, 1024], F32)
        WPN = sbw("wpn", [128, 2048], F32)
        WQBF = sbw("wqbf", [128, 1024], BF16)

        w1qt_r = W1QT[:, :].rearrange("p (k f) -> p k f", k=KD)
        w2qt_r = W2QT[:, :].rearrange("p (m dd) -> p m dd", m=NM)
        w_tr = []
        prev_vq = None
        prev_q = [None]
        bank_rd = [None, None]
        for wi, (wd, ntile, wscol) in enumerate(((w1_d, 32, 2), (w2_d, 8, 3))):
            if wi == 0:
                wr = wd.rearrange("(r p) d -> p r d", p=128)
                ncols, quarts = 1024, 1
            else:
                wr = wd.rearrange("(r p) f -> p r f", p=128)
                ncols, quarts = 4096, 4
            for r in range(ntile):
                dd = (prev_vq,) if prev_vq else ()
                t_ld = b.s(lambda e, wr=wr, r=r, ncols=ncols: e.dma_start(
                    out=WST[:, 0:ncols], in_=wr[:, r, :]), deps=dd)
                for qq in range(quarts):
                    dpn = [t_ld, t_wsc]
                    if prev_q[0]:
                        dpn.append(prev_q[0])
                    t_vq = b.a(lambda e, qq=qq, wscol=wscol: e.activation(
                        WQS[:, :], WST[:, 1024 * qq:1024 * qq + 1024], AF.Copy,
                        bias=0.0, scale=WSC[:, wscol:wscol + 1]), deps=tuple(dpn))
                    prev_vq = t_vq
                    b.v(lambda e: e.tensor_single_scalar(
                        WPN[:, 0:1024], WQS[:, :], 0.5, op=OP.is_ge), deps=(t_vq,))
                    t_n = b.v(lambda e: e.tensor_single_scalar(
                        WPN[:, 1024:2048], WQS[:, :], -0.5, op=OP.is_le))
                    t_q = b.v(lambda e: e.tensor_tensor(
                        WQBF[:, :], WPN[:, 0:1024], WPN[:, 1024:2048],
                        op=OP.subtract), deps=(t_n,))
                    prev_q[0] = None
                    for half in range(2):
                        bank = PS[half]
                        tk = None
                        for jj in range(4):
                            j = 4 * half + jj
                            dd2 = [t_q]
                            if jj == 0 and bank_rd[half]:
                                dd2.append(bank_rd[half])
                            tk = b.t(lambda e, j=j, jj=jj, bank=bank: e.matmul(
                                bank[:, 128 * jj:128 * jj + 128],
                                WQBF[:, 128 * j:128 * j + 128], IBF[:, :],
                                start=True, stop=True), deps=tuple(dd2))
                        if wi == 0:
                            def ev(e, half=half, r=r, bank=bank):
                                return e.tensor_copy(
                                    w1qt_r[:, 4 * half:4 * half + 4,
                                           128 * r:128 * r + 128],
                                    bank[:, :].rearrange("p (j t) -> p j t", j=4))
                        else:
                            def ev(e, half=half, r=r, qq=qq, bank=bank):
                                return e.tensor_copy(
                                    w2qt_r[:, 8 * qq + 4 * half:8 * qq + 4 * half + 4,
                                           128 * r:128 * r + 128],
                                    bank[:, :].rearrange("p (j t) -> p j t", j=4))
                        tv = b.v(ev, deps=(tk,))
                        bank_rd[half] = tv
                        w_tr.append(tv)
                        prev_q[0] = tv
        t_w_all = w_tr[-1]

        t_oc = b.v(lambda e: e.tensor_scalar(
            ONEC[:, :], ONF[:, 0:1], 1.0, None, op0=OP.mult), deps=(t_on,))
        tk = None
        for m in range(NM):
            for k in range(KD):
                dd = [t_w_all, t_oc] if (m == 0 and k == 0) else []
                tk = b.t(lambda e, m=m, k=k: e.matmul(
                    PS[2][:, m:m + 1],
                    W1QT[:, FF * k + 128 * m:FF * k + 128 * m + 128],
                    ONEC[:, :], start=(k == 0), stop=(k == KD - 1)), deps=tuple(dd))
        t_c1 = b.v(lambda e: e.tensor_scalar(
            CORR1[:, :], PS[2][:, 0:NM], 128.0, None, op0=OP.mult), deps=(tk,))
        tk = None
        for dch in range(KD):
            for m in range(NM):
                dd = [t_w_all, t_oc] if (dch == 0 and m == 0) else []
                tk = b.t(lambda e, dch=dch, m=m: e.matmul(
                    PS[3][:, dch:dch + 1],
                    W2QT[:, D * m + 128 * dch:D * m + 128 * dch + 128],
                    ONEC[:, :], start=(m == 0), stop=(m == NM - 1)), deps=tuple(dd))
        t_c2c = b.v(lambda e: e.tensor_scalar(
            ACC[:, 0:KD], PS[3][:, 0:KD], 128.0, None, op0=OP.mult), deps=(tk,))
        c2_last = None
        for dch in range(KD):
            dd = [t_c2c, t_if]
            if c2_last:
                dd.append(c2_last)
            vsp()
            t_d2 = b.v(lambda e, dch=dch: e.tensor_scalar(
                DGS[:, :], IF[:, :], ACC[:, dch:dch + 1], None, op0=OP.mult),
                deps=tuple(dd), hard=(t_c2c,))
            t_m2 = b.t(lambda e: e.matmul(PS[4][:, 0:128], ONF[:, :], DGS[:, :],
                                          start=True, stop=True), deps=(t_d2,))
            c2_last = b.v(lambda e, dch=dch: e.tensor_copy(
                CORR2B[:, 128 * dch:128 * dch + 128], PS[4][:, 0:128]), deps=(t_m2,))
        t_c2 = c2_last

        t_cw = b.s(lambda e: e.dma_start(
            out=CWC[:, :].rearrange("p (m t) -> p m t", m=NM),
            in_=cw_d.rearrange("(m p) t -> p m t", p=128)))
        t_cb = b.s(lambda e: e.dma_start(
            out=CBC[:, :].rearrange("p (m o) -> p m o", m=NM),
            in_=cb_d.rearrange("(m p) o -> p m o", p=128),
            allow_slow_non_contiguous=True))

        if "w1qt" in dbg:
            b.g(lambda e: e.dma_start(out=dbg["w1qt"][:, :], in_=W1QT[:, :]),
                deps=(t_w_all,))
        if "w2qt" in dbg:
            b.g(lambda e: e.dma_start(out=dbg["w2qt"][:, :], in_=W2QT[:, :]),
                deps=(t_w_all,))
        if "corr1" in dbg:
            b.g(lambda e: e.dma_start(out=dbg["corr1"][:, :], in_=CORR1[:, :]),
                deps=(t_c1,))
        if "wsc" in dbg:
            b.g(lambda e: e.dma_start(out=dbg["wsc"][:, :], in_=WSC[:, :]),
                deps=(t_wsc,))

        es_wt.close()
        es_f = ExitStack()
        sbf2 = lambda n, s, d: es_f.enter_context(nc.sbuf_tensor(n, s, d))
        XFT = sbf2("xft", [128, KD * 258], BF16)
        HT = sbf2("ht", [128, NM * 258], BF16)
        H2 = sbf2("h2", [128, NM * TT], BF16)
        H2Q = sbf2("h2q", [128, NM * TT], BF16)
        S2B = sbf2("s2b", [128, 4 * TT], F32)
        VR4 = sbf2("vr4", [128, 512], F32)
        CF4 = sbf2("cf4", [128, 512], F32)
        B4 = sbf2("b4", [128, 512], F32)
        X1T = sbf2("x1t", [128, 2 * D], F32)
        OPRE = sbf2("opre", [128, 2 * D], F32)
        OSQ = sbf2("osq", [128, D], F32)
        FCOL = sbf2("fcol", [128, 16], F32)

        out_toks = []
        osq_rd = None
        xft_free = None
        x1t_free = None
        hband_free = {}
        mm2_free = {}
        ps6_rd = None
        ps7_rd = None
        prev_sil = None
        for tau in range(NFT):
            lo = TT * tau - 1
            clo = max(lo, 0)
            chi = min(TT * tau + 257, L)
            off = clo - lo
            dd = [t_xq_all]
            if xft_free:
                dd.append(xft_free)
            t_xf = b.s(lambda e, clo=clo, chi=chi, off=off: e.dma_start(
                out=XFT[:, :].rearrange("p (k t) -> p k t", k=KD)[
                    :, :, off:off + (chi - clo)],
                in_=xq_r[:, :, clo:chi]), deps=tuple(dd))
            zp = []
            if off:
                zp.append(b.v(lambda e: e.memset(
                    XFT[:, :].rearrange("p (k t) -> p k t", k=KD)[:, :, 0:1], 0.0),
                    deps=tuple(dd)))
            if chi - clo + off < 258:
                zp.append(b.v(lambda e: e.memset(
                    XFT[:, :].rearrange("p (k t) -> p k t", k=KD)[:, :, 257:258], 0.0),
                    deps=tuple(dd)))
            if tau == 0 and "xft" in dbg:
                xft_free = b.g(lambda e: e.dma_start(
                    out=dbg["xft"][:, :], in_=XFT[:, :]), deps=tuple([t_xf] + zp))
            h_ev = {}
            for m in range(NM):
                bank = PS[m % 4]
                tk = None
                for k in range(KD):
                    def fn(e, m=m, k=k, bank=bank):
                        return e.matmul(bank[:, 0:258],
                                        W1QT[:, FF * k + 128 * m:FF * k + 128 * m + 128],
                                        XFT[:, 258 * k:258 * k + 258],
                                        start=(k == 0), stop=(k == KD - 1))
                    deps = [t_xf, t_c1, t_c2, gbr_toks[-1], t_gz] + zp
                    if m >= 4:
                        deps.append(h_ev[m - 4])
                    elif hband_free.get(m):
                        deps.append(hband_free[m])
                    tk = b.t(fn, deps=tuple(deps))
                def ev(e, m=m, tau=tau, bank=bank):
                    return e.scalar_tensor_tensor(
                        HT[:, 258 * m:258 * m + 258], bank[:, 0:258],
                        CORR1[:, m:m + 1], GBRP[:, TT * tau:TT * tau + 258],
                        op0=OP.subtract, op1=OP.mult)
                h_ev[m] = b.v(ev, deps=(tk,))
            hp = []
            if tau == 0:
                hp.append(b.v(lambda e: e.memset(
                    HT[:, :].rearrange("p (m t) -> p m t", m=NM)[:, :, 0:1], 0.0),
                    deps=(h_ev[NM - 1],)))
            if tau == NFT - 1:
                hp.append(b.v(lambda e: e.memset(
                    HT[:, :].rearrange("p (m t) -> p m t", m=NM)[:, :, 257:258], 0.0),
                    deps=(h_ev[NM - 1],)))
            cv = []
            for m in range(NM):
                hm = HT[:, 258 * m:258 * m + 258]
                deps = [h_ev[m], t_cw, t_cb] + hp
                if prev_sil:
                    deps.append(prev_sil)
                b.v(lambda e, hm=hm, m=m: e.tensor_scalar(
                    H2[:, TT * m:TT * m + TT], hm[:, 0:TT],
                    CWC[:, 3 * m:3 * m + 1], CBC[:, m:m + 1],
                    op0=OP.mult, op1=OP.add), deps=tuple(deps))
                b.v(lambda e, hm=hm, m=m: e.scalar_tensor_tensor(
                    H2[:, TT * m:TT * m + TT], hm[:, 1:TT + 1],
                    CWC[:, 3 * m + 1:3 * m + 2], H2[:, TT * m:TT * m + TT],
                    op0=OP.mult, op1=OP.add))
                cv.append(b.v(lambda e, hm=hm, m=m: e.scalar_tensor_tensor(
                    H2[:, TT * m:TT * m + TT], hm[:, 2:TT + 2],
                    CWC[:, 3 * m + 2:3 * m + 3], H2[:, TT * m:TT * m + TT],
                    op0=OP.mult, op1=OP.add)))
            t_sil = b.a(lambda e: e.activation(H2[:, :], H2[:, :], AF.Silu),
                        deps=tuple(cv))
            prev_sil = None
            t_abs = b.a(lambda e: e.activation(H2Q[:, 0:NM * TT], H2[:, :], AF.Abs),
                        deps=(t_sil,))
            t_lvl = t_abs
            width = NM
            while width > 1:
                width //= 2
                t_lvl = b.v(lambda e, width=width: e.tensor_tensor(
                    H2Q[:, 0:width * TT], H2Q[:, 0:width * TT],
                    H2Q[:, width * TT:2 * width * TT], op=OP.max), deps=(t_lvl,))
            tr2 = []
            for half in range(2):
                dd = [t_lvl]
                if ps6_rd:
                    dd.append(ps6_rd)
                tk = b.t(lambda e, half=half: e.matmul(
                    PS[6][:, 0:128], H2Q[:, 128 * half:128 * half + 128], IBF[:, :],
                    start=True, stop=True), deps=tuple(dd))
                tv = b.v(lambda e, half=half: e.reduce_max(
                    FCOL[:, half:half + 1], PS[6][:, 0:128], axis=AX), deps=(tk,))
                ps6_rd = tv
                tr2.append(tv)
            vsp()
            t_a2c = b.v(lambda e: e.tensor_single_scalar(
                FCOL[:, 0:2], FCOL[:, 0:2], 1e-5, op=OP.max), deps=tuple(tr2),
                hard=tuple(tr2))
            vsp()
            b.v(lambda e: e.reciprocal(FCOL[:, 2:3], FCOL[:, 0:1]), hard=(t_a2c,))
            vsp()
            t_rc2 = b.v(lambda e: e.reciprocal(FCOL[:, 3:4], FCOL[:, 1:2]))
            vsp()
            t_s2m = b.v(lambda e: e.tensor_single_scalar(
                FCOL[:, 2:4], FCOL[:, 2:4], 127.0, op=OP.mult), hard=(t_rc2,))
            t_g2a = b.v(lambda e: e.tensor_scalar(
                FCOL[:, 4:5], FCOL[:, 0:1], WSC[:, 1:2], None, op0=OP.mult),
                deps=(t_wsc,), hard=(t_a2c,))
            t_g2 = b.v(lambda e: e.tensor_scalar(
                FCOL[:, 5:6], FCOL[:, 1:2], WSC[:, 1:2], None, op0=OP.mult))
            t_g2m = b.v(lambda e: e.tensor_single_scalar(
                FCOL[:, 4:6], FCOL[:, 4:6], 1.0 / 127.0, op=OP.mult), hard=(t_g2,))
            s2l = None
            for half in range(2):
                dd = [t_s2m, t_if]
                if s2l:
                    dd.append(s2l)
                vsp()
                t_d = b.v(lambda e, half=half: e.tensor_scalar(
                    DGS[:, :], IF[:, :], FCOL[:, 2 + half:3 + half], None,
                    op0=OP.mult), deps=tuple(dd), hard=(t_s2m,))
                dd2 = [t_d]
                if ps6_rd:
                    dd2.append(ps6_rd)
                t_m = b.t(lambda e: e.matmul(PS[6][:, 0:128], ONF[:, :], DGS[:, :],
                                             start=True, stop=True), deps=tuple(dd2))
                s2l = b.v(lambda e, half=half: e.tensor_copy(
                    S2B[:, 128 * half:128 * half + 128], PS[6][:, 0:128]), deps=(t_m,))
                ps6_rd = s2l
            reps = [s2l]
            for j in range(1, 4):
                reps.append(b.v(lambda e, j=j: e.tensor_copy(
                    S2B[:, TT * j:TT * j + TT], S2B[:, 0:TT]), deps=(reps[0],)))
            t_s2b = reps[-1]
            q2 = []
            for g in range(NM // 2):
                t_vr = b.v(lambda e, g=g: e.tensor_tensor(
                    VR4[:, :], H2[:, 512 * g:512 * g + 512], S2B[:, 0:512],
                    op=OP.mult), deps=(t_s2b,))
                t_i2f = b.v(lambda e: e.tensor_scalar(
                    I32T[:, 0:512], VR4[:, :], 128.5, None, op0=OP.add),
                    hard=(t_vr,))
                t_cf2 = b.v(lambda e: e.tensor_copy(CF4[:, :], I32T[:, 0:512]),
                            hard=(t_i2f,))
                t_b2f = b.v(lambda e: e.scalar_tensor_tensor(
                    B4[:, :], CF4[:, :], -128.5, VR4[:, :], op0=OP.add,
                    op1=OP.is_gt), hard=(t_cf2,))
                q2.append(b.v(lambda e, g=g: e.scalar_tensor_tensor(
                    H2Q[:, 512 * g:512 * g + 512], CF4[:, :], 256.0, B4[:, :],
                    op0=OP.min, op1=OP.subtract), hard=(t_b2f,)))
            dd = [x1_toks[2 * tau + 1]]
            if x1t_free:
                dd.append(x1t_free)
            t_x1r = b.s(lambda e, tau=tau: e.dma_start(
                out=X1T[:, :].rearrange("p (c d) -> p c d", c=2),
                in_=x1r[:, 2 * tau:2 * tau + 2, :]), deps=tuple(dd))
            ad_toks = []
            for tt2 in range(2):
                ev2 = []
                for n in range(2):
                    bank = PS[4 + n]
                    tk = None
                    for m in range(NM):
                        def fn(e, tt2=tt2, n=n, m=m, bank=bank):
                            return e.matmul(
                                bank[:, :],
                                H2Q[:, TT * m + 128 * tt2:TT * m + 128 * tt2 + 128],
                                W2QT[:, D * m + 512 * n:D * m + 512 * n + 512],
                                start=(m == 0), stop=(m == NM - 1))
                        deps = [t_c2] + q2
                        if mm2_free.get(n):
                            deps.append(mm2_free[n])
                        tk = b.t(fn, deps=tuple(deps))
                    t_e1 = b.v(lambda e, bank=bank, n=n: e.tensor_tensor(
                        OSQ[:, 0:512], bank[:, :],
                        CORR2B[:, 512 * n:512 * n + 512], op=OP.subtract),
                        deps=(tk, t_g2m) + ((osq_rd,) if osq_rd else ()))
                    mm2_free[n] = t_e1
                    ev2.append(b.v(lambda e, tt2=tt2, n=n: e.tensor_scalar(
                        OPRE[:, D * tt2 + 512 * n:D * tt2 + 512 * n + 512],
                        OSQ[:, 0:512], FCOL[:, 4 + tt2:5 + tt2], None, op0=OP.mult)))
                t_ad = b.v(lambda e, tt2=tt2: e.tensor_tensor(
                    OPRE[:, D * tt2:D * tt2 + D], OPRE[:, D * tt2:D * tt2 + D],
                    X1T[:, D * tt2:D * tt2 + D], op=OP.add),
                    deps=tuple(ev2) + (t_x1r,))
                ad_toks.append(t_ad)
                t_sm = b.a(lambda e, tt2=tt2: e.activation(
                    OSQ[:, :], OPRE[:, D * tt2:D * tt2 + D], AF.Copy,
                    accum_out=FCOL[:, 8:9]), deps=(t_ad,))
                t_mu = b.v(lambda e: e.tensor_scalar(
                    FCOL[:, 8:9], FCOL[:, 8:9], 1.0 / D, None, op0=OP.mult),
                    deps=(t_sm,))
                vsp()
                t_dd = b.v(lambda e, tt2=tt2: e.tensor_scalar(
                    OPRE[:, D * tt2:D * tt2 + D], OPRE[:, D * tt2:D * tt2 + D],
                    FCOL[:, 8:9], None, op0=OP.subtract), hard=(t_mu,))
                t_sq = b.a(lambda e, tt2=tt2: e.activation(
                    OSQ[:, :], OPRE[:, D * tt2:D * tt2 + D], AF.Square,
                    accum_out=FCOL[:, 9:10]), deps=(t_dd,))
                osq_rd = t_sq
                t_v2 = b.v(lambda e: e.tensor_scalar(
                    FCOL[:, 10:11], FCOL[:, 9:10], 1.0 / D, 1e-5,
                    op0=OP.mult, op1=OP.add), deps=(t_sq,))
                t_sd = b.a(lambda e: e.activation(
                    FCOL[:, 12:13], FCOL[:, 10:11], AF.Sqrt), deps=(t_v2,))
                vsp()
                t_rs = b.v(lambda e: e.reciprocal(FCOL[:, 11:12], FCOL[:, 12:13]),
                           deps=(t_sd,))
                vsp()
                t_fin = b.v(lambda e, tt2=tt2: e.tensor_scalar(
                    OPRE[:, D * tt2:D * tt2 + D], OPRE[:, D * tt2:D * tt2 + D],
                    FCOL[:, 11:12], None, op0=OP.mult), deps=(t_rs,), hard=(t_rs,))
                out_toks.append(b.s(lambda e, tau=tau, tt2=tt2: e.dma_start(
                    out=out_r[:, 2 * tau + tt2, :], in_=OPRE[:, D * tt2:D * tt2 + D]),
                    deps=(t_fin,)))
            x1t_free = ad_toks[-1]
            xft_free = h_ev[NM - 1]
            prev_sil = t_sil
            for m in range(4):
                hband_free[m] = cv[NM - 4 + m]
            if tau == 0 and "ht" in dbg:
                tkd = b.g(lambda e: e.dma_start(out=dbg["ht"][:, :], in_=HT[:, :]),
                          deps=(cv[-1],))
                for m in range(NM):
                    hband_free[m % 4] = tkd
                xft_free = tkd
            if tau == 0 and "h2q" in dbg:
                tkd = b.g(lambda e: e.dma_start(out=dbg["h2q"][:, :], in_=H2Q[:, :]),
                          deps=tuple(q2) + (out_toks[-1],))
                prev_sil = tkd if False else prev_sil
                hband_free[0] = tkd

        b.emit("G", lambda e: e.nop(), deps=(out_toks[-1],), inc=0)

        with nc.Block() as block:
            b.build(block, sems)

        es_f.close()
        es_corr.close()
        es_wq.close()
        es_late.close()
    es_glob.close()
    return nc


_NC_CACHE = {}


def kernel(L0_attn_x, L0_attn_Wq, L0_attn_Wk, L0_attn_Wv, L0_attn_Wb,
           L0_attn_ln_w, L0_attn_ln_b, L0_ffn_w1, L0_ffn_conv_w,
           L0_ffn_conv_b, L0_ffn_w2):
    x = np.ascontiguousarray(np.asarray(L0_attn_x, np.float32))
    B = x.shape[0]
    base = {
        "wq": np.ascontiguousarray(np.asarray(L0_attn_Wq, np.float32)),
        "wk": np.ascontiguousarray(np.asarray(L0_attn_Wk, np.float32)),
        "wv": np.ascontiguousarray(np.asarray(L0_attn_Wv, np.float32)),
        "wb": np.ascontiguousarray(np.asarray(L0_attn_Wb, np.float32)),
        "lnw": np.ascontiguousarray(np.asarray(L0_attn_ln_w, np.float32).reshape(1, D)),
        "lnb": np.ascontiguousarray(np.asarray(L0_attn_ln_b, np.float32).reshape(1, D)),
        "w1": np.ascontiguousarray(np.asarray(L0_ffn_w1, np.float32)),
        "cw": np.ascontiguousarray(np.asarray(L0_ffn_conv_w, np.float32).reshape(FF, 3)),
        "cb": np.ascontiguousarray(np.asarray(L0_ffn_conv_b, np.float32).reshape(FF, 1)),
        "w2": np.ascontiguousarray(np.asarray(L0_ffn_w2, np.float32)),
        **const_inputs(),
    }
    if "main" not in _NC_CACHE:
        _NC_CACHE["main"] = build_nc()
    nc = _NC_CACHE["main"]
    in_maps = [dict(base, x=np.ascontiguousarray(x[bb])) for bb in range(B)]
    res = run_bass_kernel_spmd(nc, in_maps, core_ids=list(range(B)))
    out = np.stack([np.asarray(r["out"]) for r in res.results], axis=0)
    return out.astype(np.float32)
